# version 13
# baseline (speedup 1.0000x reference)
import contextlib
import numpy as np

import concourse.bass as bass
import concourse.mybir as mybir
from concourse.bass_utils import run_bass_kernel_spmd

F32 = mybir.dt.float32
BF16 = mybir.dt.bfloat16
F32R = mybir.dt.float32r
AF = mybir.ActivationFunctionType
ALU = mybir.AluOpType

D = 1024
T = 4096
CH = 512
NCH = T // CH
SUB = 128
NSUB = CH // SUB
NH = 16
HD = 64
NFB = D // 128
EPS = 1e-5


class _Sem:
    __slots__ = ("h", "count")

    def __init__(self):
        self.h = None
        self.count = 0


class _Eng:
    __slots__ = ("name", "sem", "count")

    def __init__(self, name):
        self.name = name
        self.sem = None
        self.count = 0


def build_program():
    nc = bass.Bass("TRN2", target_bir_lowering=False)

    d_xT = nc.dram_tensor("xT", [D, T], F32, kind="ExternalInput")
    d_wq = nc.dram_tensor("wq", [D, D], F32, kind="ExternalInput")
    d_wrt = nc.dram_tensor("wrt", [D, D], F32, kind="ExternalInput")
    d_wit = nc.dram_tensor("wit", [D, D], F32, kind="ExternalInput")
    d_bias = nc.dram_tensor("bias", [D], F32, kind="ExternalInput")
    d_cosh = nc.dram_tensor("cosh", [D, D], BF16, kind="ExternalInput")
    d_cosl = nc.dram_tensor("cosl", [D, D], BF16, kind="ExternalInput")
    d_sinh = nc.dram_tensor("sinh", [D, D], BF16, kind="ExternalInput")
    d_sinl = nc.dram_tensor("sinl", [D, D], BF16, kind="ExternalInput")
    d_mask = nc.dram_tensor("mask128", [128, 128], F32, kind="ExternalInput")
    d_id = nc.dram_tensor("ident", [128, 128], F32, kind="ExternalInput")
    d_out = nc.dram_tensor("outT", [D, T], F32, kind="ExternalOutput")

    PE, VE, SC, SY, GP = (_Eng(n) for n in ("pe", "ve", "sc", "sy", "gp"))
    ALL = [PE, VE, SC, SY, GP]

    last_w = {}
    readers = {}

    def _deps(reads, writes):
        dep = {}

        def add(tok):
            if tok:
                e, v = tok
                if dep.get(e, 0) < v:
                    dep[e] = v
        for r in reads:
            add(last_w.get(r))
        for r in writes:
            add(last_w.get(r))
            for e, v in readers.get(r, {}).items():
                add((e, v))
        return dep

    def _commit(tok_src, reads, writes):
        tok = (tok_src, tok_src.count)
        for r in writes:
            last_w[r] = tok
            readers[r] = {}
        for r in reads:
            readers.setdefault(r, {})[tok_src] = tok_src.count

    def _sem_of(x):
        return x.sem if isinstance(x, _Eng) else x.h

    with contextlib.ExitStack() as S0:
        E0 = S0.enter_context
        ps_S = E0(nc.psum_tensor("ps_S", [128, 512], F32))
        ps_scr = E0(nc.psum_tensor("ps_scr", [128, 2048], F32))
        ps_att = E0(nc.psum_tensor("ps_att", [128, 1024], F32))
        ps_rot = E0(nc.psum_tensor("ps_rot", [128, 512], F32))
        for e in ALL:
            e.sem = E0(nc.semaphore("sem_" + e.name))
        _pool_n = [0]

        def E0s():
            s = _Sem()
            _pool_n[0] += 1
            s.h = E0(nc.semaphore("dsem_%d" % _pool_n[0]))
            return s
        block = E0(nc.Block())
        DEC = {"pe": block.tensor, "ve": block.vector, "sc": block.scalar,
               "sy": block.sync, "gp": block.gpsimd}

        NPOOL = 10
        dpool = {"sy": [E0s() for _ in range(NPOOL)],
                 "gp": [E0s() for _ in range(NPOOL)]}
        dpool_i = {"sy": 0, "gp": 0}

        def cstep(eng, emit, reads=(), writes=()):
            dep = _deps(reads, writes)

            @DEC[eng.name]
            def _(e):
                for o, v in dep.items():
                    e.wait_ge(_sem_of(o), v)
                emit(e).then_inc(eng.sem, 1)
            eng.count += 1
            _commit(eng, reads, writes)

        def dstep(eng, transfers, reads=(), writes=()):
            dep = _deps(reads, writes)
            sem = dpool[eng.name][dpool_i[eng.name] % NPOOL]
            dpool_i[eng.name] += 1
            prior = sem.count

            @DEC[eng.name]
            def _(e):
                for o, v in dep.items():
                    e.wait_ge(_sem_of(o), v)
                if prior:
                    e.wait_ge(sem.h, prior)
                for tr_ in transfers:
                    dst, src = tr_[0], tr_[1]
                    kw = tr_[2] if len(tr_) > 2 else {}
                    e.dma_start(dst, src, **kw).then_inc(sem.h, 16)
            sem.count += 16 * len(transfers)
            _commit(sem, reads, writes)

        slots = [(None, "psR"), (None, "psA0"), (None, "psA1")]

        sMh = E0(nc.sbuf_tensor("sMh", [128, 8192], BF16))
        sMl = E0(nc.sbuf_tensor("sMl", [128, 8192], BF16))
        s_ones = E0(nc.sbuf_tensor("s_ones", [128, 128], F32))
        s_epsc = E0(nc.sbuf_tensor("s_epsc", [128, 1], F32))
        slots[0] = (ps_rot[:, 0:512], "psR")
        slots[1] = (ps_att[:, 0:512], "psA0")
        slots[2] = (ps_att[:, 512:1024], "psA1")

        cstep(VE, lambda e: (e.memset(s_ones[:, :], 1.0),
                             e.memset(s_epsc[:, :], EPS))[-1], writes=["s_ones"])
        cstep(SC, lambda e: e.preload_act_table(AF.Sqrt), writes=["acttab"])

        with contextlib.ExitStack() as S1:
            E1 = S1.enter_context
            sCh = E1(nc.sbuf_tensor("sCh", [128, 8192], BF16))
            sCl = E1(nc.sbuf_tensor("sCl", [128, 8192], BF16))
            sNh = E1(nc.sbuf_tensor("sNh", [128, 8192], BF16))
            sNl = E1(nc.sbuf_tensor("sNl", [128, 8192], BF16))
            sRh = E1(nc.sbuf_tensor("sRh", [128, 8192], BF16))
            sRl = E1(nc.sbuf_tensor("sRl", [128, 8192], BF16))
            sIh = E1(nc.sbuf_tensor("sIh", [128, 8192], BF16))
            sIl = E1(nc.sbuf_tensor("sIl", [128, 8192], BF16))
            sWtmp = E1(nc.sbuf_tensor("sWtmp", [128, 8192], F32))

            for src_, dst, rn in ((d_cosh, sCh, "sCh"), (d_cosl, sCl, "sCl"),
                                  (d_sinh, sNh, "sNh"), (d_sinl, sNl, "sNl")):
                dstep(SY, [(dst[:, 1024 * j:1024 * (j + 1)],
                            src_[128 * j:128 * (j + 1), :]) for j in range(8)],
                      writes=[rn])

            for dsrc, hi, lo, hn, ln_ in ((d_wrt, sRh, sRl, "sRh", "sRl"),
                                          (d_wit, sIh, sIl, "sIh", "sIl")):
                dstep(SY, [(sWtmp[:, 1024 * j:1024 * (j + 1)],
                            dsrc[128 * j:128 * (j + 1), :]) for j in range(8)],
                      writes=["sWtmp"])

                def caph(e, hi=hi):
                    for j in range(8):
                        last = e.activation(hi[:, 1024 * j:1024 * (j + 1)],
                                            sWtmp[:, 1024 * j:1024 * (j + 1)], AF.Identity)
                    return last
                cstep(SC, caph, reads=["sWtmp", "acttab"], writes=[hn])

                def capl(e, hi=hi, lo=lo):
                    for j in range(8):
                        last = e.scalar_tensor_tensor(
                            lo[:, 1024 * j:1024 * (j + 1)],
                            sWtmp[:, 1024 * j:1024 * (j + 1)], 1.0,
                            hi[:, 1024 * j:1024 * (j + 1)],
                            op0=ALU.mult, op1=ALU.subtract)
                    return last
                cstep(VE, capl, reads=["sWtmp", hn], writes=[ln_])

            rh3 = sRh[:, :].rearrange("p (ob d) -> p ob d", d=1024)
            rl3 = sRl[:, :].rearrange("p (ob d) -> p ob d", d=1024)
            ih3 = sIh[:, :].rearrange("p (ob d) -> p ob d", d=1024)
            il3 = sIl[:, :].rearrange("p (ob d) -> p ob d", d=1024)
            pslots = [(ps_rot[:, 0:512], "psR"), (ps_att[:, 0:512], "psA0"),
                      (ps_att[:, 512:1024], "psA1"), (ps_scr[:, 0:512], "psS0")]

            def tri(e, slot, ah, al, bh, bl, first, last_):
                for (x, y, st, sp) in ((ah, bh, first, False),
                                       (ah, bl, False, False),
                                       (al, bh, False, last_)):
                    for ob in range(8):
                        r = e.matmul(slot, x(ob), y(ob),
                                     start=(st and ob == 0),
                                     stop=(sp and ob == 7))
                return r

            for j in range(8):
                def mmPr(e, j=j):
                    for half in range(2):
                        wr_h = lambda ob, j=j: sRh[:, 1024 * ob + 128 * j:1024 * ob + 128 * (j + 1)]
                        wr_l = lambda ob, j=j: sRl[:, 1024 * ob + 128 * j:1024 * ob + 128 * (j + 1)]
                        wi_h = lambda ob, j=j: sIh[:, 1024 * ob + 128 * j:1024 * ob + 128 * (j + 1)]
                        wi_l = lambda ob, j=j: sIl[:, 1024 * ob + 128 * j:1024 * ob + 128 * (j + 1)]
                        c_h = lambda ob, half=half: sCh[:, 1024 * ob + 512 * half:1024 * ob + 512 * (half + 1)]
                        c_l = lambda ob, half=half: sCl[:, 1024 * ob + 512 * half:1024 * ob + 512 * (half + 1)]
                        n_h = lambda ob, half=half: sNh[:, 1024 * ob + 512 * half:1024 * ob + 512 * (half + 1)]
                        n_l = lambda ob, half=half: sNl[:, 1024 * ob + 512 * half:1024 * ob + 512 * (half + 1)]
                        tri(e, pslots[half][0], wr_h, wr_l, c_h, c_l, True, False)
                        last = tri(e, pslots[half][0], wi_h, wi_l, n_h, n_l, False, True)
                    return last
                cstep(PE, mmPr, reads=["sRh", "sRl", "sIh", "sIl",
                                       "sCh", "sCl", "sNh", "sNl"],
                      writes=["psR", "psA0"])

                cstep(GP, lambda e, j=j: (e.tensor_scalar_mul(
                    ih3[:, :, 128 * j:128 * (j + 1)],
                    ih3[:, :, 128 * j:128 * (j + 1)], -1.0),
                    e.tensor_scalar_mul(
                    il3[:, :, 128 * j:128 * (j + 1)],
                    il3[:, :, 128 * j:128 * (j + 1)], -1.0))[-1],
                    reads=["sIh", "sIl"], writes=["sIh", "sIl"])

                def mmPi(e, j=j):
                    for half in range(2):
                        wr_h = lambda ob, j=j: sRh[:, 1024 * ob + 128 * j:1024 * ob + 128 * (j + 1)]
                        wr_l = lambda ob, j=j: sRl[:, 1024 * ob + 128 * j:1024 * ob + 128 * (j + 1)]
                        wi_h = lambda ob, j=j: sIh[:, 1024 * ob + 128 * j:1024 * ob + 128 * (j + 1)]
                        wi_l = lambda ob, j=j: sIl[:, 1024 * ob + 128 * j:1024 * ob + 128 * (j + 1)]
                        c_h = lambda ob, half=half: sCh[:, 1024 * ob + 512 * half:1024 * ob + 512 * (half + 1)]
                        c_l = lambda ob, half=half: sCl[:, 1024 * ob + 512 * half:1024 * ob + 512 * (half + 1)]
                        n_h = lambda ob, half=half: sNh[:, 1024 * ob + 512 * half:1024 * ob + 512 * (half + 1)]
                        n_l = lambda ob, half=half: sNl[:, 1024 * ob + 512 * half:1024 * ob + 512 * (half + 1)]
                        tri(e, pslots[2 + half][0], wi_h, wi_l, c_h, c_l, True, False)
                        last = tri(e, pslots[2 + half][0], wr_h, wr_l, n_h, n_l, False, True)
                    return last
                cstep(PE, mmPi, reads=["sRh", "sRl", "sIh", "sIl",
                                       "sCh", "sCl", "sNh", "sNl"],
                      writes=["psA1", "psS0"])

                def evh(e, j=j):
                    for half in range(2):
                        e.activation(rh3[:, 4 * half:4 * half + 4, 128 * j:128 * (j + 1)],
                                     pslots[half][0].rearrange("p (o c) -> p o c", c=128),
                                     AF.Identity)
                        last = e.activation(ih3[:, 4 * half:4 * half + 4, 128 * j:128 * (j + 1)],
                                            pslots[2 + half][0].rearrange("p (o c) -> p o c", c=128),
                                            AF.Identity)
                    return last
                cstep(SC, evh, reads=["psR", "psA0", "psA1", "psS0", "acttab"],
                      writes=["sRh", "sIh"])

                def evl(e, j=j):
                    for half in range(2):
                        e.scalar_tensor_tensor(
                            rl3[:, 4 * half:4 * half + 4, 128 * j:128 * (j + 1)],
                            pslots[half][0].rearrange("p (o c) -> p o c", c=128), 1.0,
                            rh3[:, 4 * half:4 * half + 4, 128 * j:128 * (j + 1)],
                            op0=ALU.mult, op1=ALU.subtract)
                        last = e.scalar_tensor_tensor(
                            il3[:, 4 * half:4 * half + 4, 128 * j:128 * (j + 1)],
                            pslots[2 + half][0].rearrange("p (o c) -> p o c", c=128), 1.0,
                            ih3[:, 4 * half:4 * half + 4, 128 * j:128 * (j + 1)],
                            op0=ALU.mult, op1=ALU.subtract)
                    return last
                cstep(VE, evl, reads=["psR", "psA0", "psA1", "psS0", "sRh", "sIh"],
                      writes=["sRl", "sIl"])

            for ab in range(8):
                for half in range(2):
                    slot, sn = slots[(2 * ab + half) % 3]

                    def mmM(e, ab=ab, half=half, slot=slot):
                        c_h = lambda kb, ab=ab: sCh[:, 1024 * kb + 128 * ab:1024 * kb + 128 * (ab + 1)]
                        c_l = lambda kb, ab=ab: sCl[:, 1024 * kb + 128 * ab:1024 * kb + 128 * (ab + 1)]
                        n_h = lambda kb, ab=ab: sNh[:, 1024 * kb + 128 * ab:1024 * kb + 128 * (ab + 1)]
                        n_l = lambda kb, ab=ab: sNl[:, 1024 * kb + 128 * ab:1024 * kb + 128 * (ab + 1)]
                        pr_h = lambda kb, half=half: rh3[:, 4 * half:4 * half + 4, 128 * kb:128 * (kb + 1)]
                        pr_l = lambda kb, half=half: rl3[:, 4 * half:4 * half + 4, 128 * kb:128 * (kb + 1)]
                        pi_h = lambda kb, half=half: ih3[:, 4 * half:4 * half + 4, 128 * kb:128 * (kb + 1)]
                        pi_l = lambda kb, half=half: il3[:, 4 * half:4 * half + 4, 128 * kb:128 * (kb + 1)]
                        tri(e, slot, c_h, c_l, pr_h, pr_l, True, False)
                        return tri(e, slot, n_h, n_l, pi_h, pi_l, False, True)
                    cstep(PE, mmM, reads=["sCh", "sCl", "sNh", "sNl",
                                          "sRh", "sRl", "sIh", "sIl"], writes=[sn])
                    cstep(SC, lambda e, ab=ab, half=half, slot=slot: e.activation(
                        sMh[:, 1024 * ab + 512 * half:1024 * ab + 512 * (half + 1)],
                        slot, AF.Identity, scale=1.0 / D),
                        reads=[sn, "acttab"], writes=["sMh"])
                    cstep(VE, lambda e, ab=ab, half=half, slot=slot: e.scalar_tensor_tensor(
                        sMl[:, 1024 * ab + 512 * half:1024 * ab + 512 * (half + 1)],
                        slot, 1.0 / D,
                        sMh[:, 1024 * ab + 512 * half:1024 * ab + 512 * (half + 1)],
                        op0=ALU.mult, op1=ALU.subtract),
                        reads=[sn, "sMh"], writes=["sMl"])

        prep_res = ("sCh", "sCl", "sNh", "sNl", "sRh", "sRl", "sIh", "sIl", "sWtmp")
        bar = _deps(prep_res, prep_res)
        steady_names = ["swqh", "swql", "sbias", "smask", "sid", "sS", "murep", "isrep",
                        "ftmp0", "ftmp1", "sqt0", "sqt1",
                        "bufA:a", "bufA:b", "bufB:a", "bufB:b", "bufC:a", "bufC:b",
                        "bufE:a", "bufE:b", "bufF", "bufG:a", "bufG:b",
                        "bufH:a", "bufH:b", "bufI:a", "bufI:b",
                        "scsb:0", "scsb:1", "iscol"]
        for rname in steady_names:
            readers[rname] = dict(bar)

        swqh = E0(nc.sbuf_tensor("swqh", [128, 8192], BF16))
        swql = E0(nc.sbuf_tensor("swql", [128, 8192], BF16))
        sbias = E0(nc.sbuf_tensor("sbias", [128, 8], F32))
        smask = E0(nc.sbuf_tensor("smask", [128, 128], F32))
        sid = E0(nc.sbuf_tensor("sid", [128, 128], F32))
        sS = E0(nc.sbuf_tensor("sS", [128, 512], F32))
        murep = E0(nc.sbuf_tensor("murep", [128, 512], F32))
        isrep = E0(nc.sbuf_tensor("isrep", [128, 512], F32))
        ftmp = [E0(nc.sbuf_tensor("ftmp0", [128, 512], F32)),
                E0(nc.sbuf_tensor("ftmp1", [128, 512], F32))]
        sqt0 = E0(nc.sbuf_tensor("sqt0", [128, 512], F32))
        sqt1 = E0(nc.sbuf_tensor("sqt1", [128, 512], F32))
        bufA = E0(nc.sbuf_tensor("bufA", [128, NFB * CH], F32))
        bufB = E0(nc.sbuf_tensor("bufB", [128, NFB * CH], F32))
        bufC = E0(nc.sbuf_tensor("bufC", [128, NFB * CH], F32))
        bufE = E0(nc.sbuf_tensor("bufE", [128, NFB * CH], F32))
        bufF = E0(nc.sbuf_tensor("bufF", [128, NSUB * D], F32))
        bufG = E0(nc.sbuf_tensor("bufG", [128, NSUB * D], F32))
        bufH = E0(nc.sbuf_tensor("bufH", [128, NFB * CH], BF16))
        bufI = E0(nc.sbuf_tensor("bufI", [128, NFB * CH], BF16))
        scsb = E0(nc.sbuf_tensor("scsb", [128, 2048], F32))
        iscol = E0(nc.sbuf_tensor("iscol", [128, NSUB], F32))

        dstep(SY, [(bufA[:, 1024 * j:1024 * (j + 1)], d_wq[128 * j:128 * (j + 1), :])
                   for j in range(4)], writes=["bufA:a", "bufA:b"])
        dstep(SY, [(bufB[:, 1024 * j:1024 * (j + 1)], d_wq[128 * (4 + j):128 * (5 + j), :])
                   for j in range(4)], writes=["bufB:a", "bufB:b"])

        def wqh(e):
            for j in range(8):
                srcb = (bufA if j < 4 else bufB)[:, 1024 * (j % 4):1024 * (j % 4 + 1)]
                last = e.activation(swqh[:, 1024 * j:1024 * (j + 1)], srcb, AF.Identity)
            return last
        cstep(SC, wqh, reads=["bufA:a", "bufA:b", "bufB:a", "bufB:b", "acttab"],
              writes=["swqh"])

        def wql(e):
            for j in range(8):
                srcb = (bufA if j < 4 else bufB)[:, 1024 * (j % 4):1024 * (j % 4 + 1)]
                last = e.scalar_tensor_tensor(
                    swql[:, 1024 * j:1024 * (j + 1)], srcb, 1.0,
                    swqh[:, 1024 * j:1024 * (j + 1)], op0=ALU.mult, op1=ALU.subtract)
            return last
        cstep(VE, wql, reads=["bufA:a", "bufA:b", "bufB:a", "bufB:b", "swqh"],
              writes=["swql"])

        dstep(SY, [(sbias[:, :], d_bias.ap().rearrange("(b p) -> p b", p=128),
                    {"allow_slow_non_contiguous": True}),
                   (smask[:, :], d_mask.ap()),
                   (sid[:, :], d_id.ap())],
              writes=["sbias", "smask", "sid"])

        cstep(VE, lambda e: e.tensor_scalar_mul(sbias[:, :], sbias[:, :], 0.5),
              reads=["sbias"], writes=["sbias"])
        cstep(VE, lambda e: e.memset(scsb[:, :], 0.0),
              writes=["scsb:0", "scsb:1"])
        cstep(VE, lambda e: e.memset(ps_scr[:, :], 0.0),
              writes=["ps_scr:0", "ps_scr:1"])

        def fb(buf, i, t0=0, t1=CH):
            return buf[:, CH * i + t0:CH * i + t1]

        def layer_norm(zF, zn, nF, nn, make_iscol=False):
            for h_ in range(2):
                def sq(e, h_=h_):
                    for i in range(4 * h_, 4 * h_ + 4):
                        last = e.activation(fb(bufG, i), fb(zF, i), AF.Square)
                    return last
                cstep(SC, sq, reads=[zn + ":" + "ab"[h_], "acttab"],
                      writes=["bufG:" + "ab"[h_]])

            z4 = zF[:, :].rearrange("p (f t) -> p t f", t=CH)
            g4 = bufG[:, :].rearrange("p (f t) -> p t f", t=CH)
            cstep(VE, lambda e: e.tensor_reduce(
                sqt0[:, :].unsqueeze(2), z4, axis=mybir.AxisListType.X, op=ALU.add),
                reads=[zn + ":a", zn + ":b"], writes=["sqt0"])
            cstep(PE, lambda e: e.matmul(ps_rot[:, 0:512], s_ones[:, :], sqt0[:, :],
                                         start=True, stop=True),
                  reads=["sqt0", "s_ones"], writes=["psR"])
            cstep(VE, lambda e: e.tensor_reduce(
                sqt1[:, :].unsqueeze(2), g4, axis=mybir.AxisListType.X, op=ALU.add),
                reads=["bufG:a", "bufG:b"], writes=["sqt1"])
            cstep(PE, lambda e: e.matmul(ps_att[:, 0:512], s_ones[:, :], sqt1[:, :],
                                         start=True, stop=True),
                  reads=["sqt1", "s_ones"], writes=["psA0"])

            cstep(SC, lambda e: e.activation(murep[:, :], ps_rot[:, 0:512],
                                             AF.Identity, scale=1.0 / D),
                  reads=["psR", "acttab"], writes=["murep"])
            cstep(VE, lambda e: e.tensor_tensor(isrep[:, :], murep[:, :], murep[:, :],
                                                op=ALU.mult),
                  reads=["murep"], writes=["isrep"])
            cstep(VE, lambda e: e.scalar_tensor_tensor(
                isrep[:, :], ps_att[:, 0:512], 1.0 / D, isrep[:, :],
                op0=ALU.mult, op1=ALU.subtract),
                reads=["psA0", "isrep"], writes=["isrep"])
            cstep(SC, lambda e: e.activation(isrep[:, :], isrep[:, :], AF.Identity,
                                             bias=s_epsc[:, 0:1]),
                  reads=["isrep", "acttab"], writes=["isrep"])
            cstep(VE, lambda e: e.reciprocal(isrep[:, :], isrep[:, :]),
                  reads=["isrep"], writes=["isrep"])
            cstep(SC, lambda e: e.activation(isrep[:, :], isrep[:, :], AF.Sqrt),
                  reads=["isrep", "acttab"], writes=["isrep"])

            z3 = zF[:, :].rearrange("p (f t) -> p f t", t=CH)
            n3 = nF[:, :].rearrange("p (f t) -> p f t", t=CH)
            mrep = murep[:, :].unsqueeze(1).broadcast_to([128, 4, CH])
            for h_ in range(2):
                s_ = slice(4 * h_, 4 * h_ + 4)
                tag = ":" + "ab"[h_]
                cstep(VE, lambda e, s_=s_: e.tensor_tensor(
                    n3[:, s_, :], z3[:, s_, :], mrep, op=ALU.subtract),
                    reads=[zn + tag, "murep"], writes=[nn + tag])
                cstep(SC, lambda e, h_=h_: e.activation(
                    bufH[:, 2048 * h_:2048 * (h_ + 1)],
                    nF[:, 2048 * h_:2048 * (h_ + 1)], AF.Identity),
                    reads=[nn + tag, "acttab"], writes=["bufH" + tag])
                cstep(VE, lambda e, h_=h_: e.scalar_tensor_tensor(
                    bufI[:, 2048 * h_:2048 * (h_ + 1)],
                    nF[:, 2048 * h_:2048 * (h_ + 1)], 1.0,
                    bufH[:, 2048 * h_:2048 * (h_ + 1)],
                    op0=ALU.mult, op1=ALU.subtract),
                    reads=[nn + tag, "bufH" + tag], writes=["bufI" + tag])
            if make_iscol:
                def istr(e):
                    for s in range(NSUB):
                        last = e.matmul(ps_scr[:, s:s + 1],
                                        isrep[0:1, SUB * s:SUB * (s + 1)],
                                        sid[0:1, 0:1], is_transpose=True,
                                        start=(s == 0), stop=(s == NSUB - 1))
                    return last
                cstep(PE, istr, reads=["isrep", "sid"], writes=["ps_scr:0"])
                cstep(SC, lambda e: e.copy(iscol[:, :], ps_scr[:, 0:NSUB]),
                      reads=["ps_scr:0", "acttab"], writes=["iscol"])

        def dense(wh, wl, wns, outF, on, resF, rn, with_bias_half):
            for db in range(NFB):
                slot, sn = slots[db % 3]
                otag = on + (":a" if db < 4 else ":b")

                def mm1(e, db=db, slot=slot):
                    for kb in range(4):
                        last = e.matmul(
                            slot,
                            wh[:, 1024 * kb + 128 * db:1024 * kb + 128 * (db + 1)],
                            fb(bufH, kb), start=(kb == 0), stop=False)
                    return last
                cstep(PE, mm1, reads=["bufH:a"] + wns, writes=[sn])

                def mm2(e, db=db, slot=slot):
                    for kb in range(4, 8):
                        e.matmul(slot,
                                 wh[:, 1024 * kb + 128 * db:1024 * kb + 128 * (db + 1)],
                                 fb(bufH, kb), start=False, stop=False)
                    for wt, nt, sp in ((wh, bufI, False), (wl, bufH, True)):
                        for kb in range(NFB):
                            last = e.matmul(
                                slot,
                                wt[:, 1024 * kb + 128 * db:1024 * kb + 128 * (db + 1)],
                                fb(nt, kb), start=False,
                                stop=(sp and kb == NFB - 1))
                    return last
                cstep(PE, mm2, reads=["bufH:a", "bufH:b", "bufI:a", "bufI:b"] + wns,
                      writes=[sn])
                if with_bias_half:
                    ft, fn = ftmp[db % 2], "ftmp%d" % (db % 2)
                    cstep(VE, lambda e, db=db, slot=slot, ft=ft: e.scalar_tensor_tensor(
                        ft[:, :], slot, 0.5, isrep[:, :], op0=ALU.mult, op1=ALU.mult),
                        reads=[sn, "isrep"], writes=[fn])
                    cstep(VE, lambda e, db=db, ft=ft: e.scalar_tensor_tensor(
                        fb(outF, db), ft[:, :], sbias[:, db:db + 1], fb(resF, db),
                        op0=ALU.add, op1=ALU.add),
                        reads=[fn, rn + (":a" if db < 4 else ":b"), "sbias"],
                        writes=[otag])
                else:
                    cstep(VE, lambda e, db=db, slot=slot: e.scalar_tensor_tensor(
                        fb(outF, db), slot, 1.0, isrep[:, :], op0=ALU.mult, op1=ALU.mult),
                        reads=[sn, "isrep"], writes=[otag])

        for c in range(NCH):
            tc = CH * c

            dstep(SY, [(fb(bufA, i), d_xT[128 * i:128 * (i + 1), tc:tc + CH])
                       for i in range(NFB)], writes=["bufA:a", "bufA:b"])

            layer_norm(bufA, "bufA", bufB, "bufB")
            dense(sMh, sMl, ["sMh", "sMl"], bufC, "bufC", bufA, "bufA", True)

            layer_norm(bufC, "bufC", bufB, "bufB", make_iscol=True)
            dense(swqh, swql, ["swqh", "swql"], bufE, "bufE", None, None, False)

            for sb in range(NSUB):
                def tr(e, sb=sb):
                    for i in range(NFB):
                        last = e.matmul(
                            ps_scr[:, 128 * i:128 * (i + 1)],
                            bufB[:, CH * i + SUB * sb:CH * i + SUB * (sb + 1)],
                            sid[:, :], is_transpose=True,
                            start=(i % 4 == 0), stop=(i % 4 == 3))
                    return last
                cstep(PE, tr, reads=["bufB:a", "bufB:b", "sid"], writes=["ps_scr:0"])

                def trev(e, sb=sb):
                    e.activation(bufF[:, D * sb:D * sb + 512], ps_scr[:, 0:512],
                                 AF.Identity, scale=iscol[:, sb:sb + 1])
                    return e.activation(bufF[:, D * sb + 512:D * sb + 1024],
                                        ps_scr[:, 512:1024],
                                        AF.Identity, scale=iscol[:, sb:sb + 1])
                cstep(SC, trev, reads=["ps_scr:0", "iscol", "acttab"], writes=["bufF"])

            scr3 = ps_scr[:, :].rearrange("p (h c) -> p h c", c=128)
            ssb3 = scsb[:, :].rearrange("p (h c) -> p h c", c=128)
            for sb in range(NSUB):
                g = NSUB * c + sb

                def sc_head(e, h, sb):
                    p, half = h // 2, h % 2
                    kv = bufB[64 * half:64 * half + 64,
                              CH * p + SUB * sb:CH * p + SUB * sb + 128]
                    q = bufE[64 * half:64 * half + 64,
                             CH * p + SUB * sb:CH * p + SUB * sb + 128]
                    return e.matmul(ps_scr[0:128, 128 * h:128 * h + 128], kv, q,
                                    start=(h % 4 == 0), stop=(h % 4 == 3),
                                    tile_position=(64 * half, 0))

                def sc_e(e, sb=sb):
                    for h in range(0, NH, 2):
                        last = sc_head(e, h, sb)
                    return last
                cstep(PE, sc_e, reads=["bufB:a", "bufB:b", "bufE:a", "bufE:b"],
                      writes=["ps_scr:0", "ps_scr:1"])

                def sc_o0(e, sb=sb):
                    e.drain()
                    for h in (1, 3, 5, 7):
                        last = sc_head(e, h, sb)
                    return last
                cstep(PE, sc_o0, reads=["bufB:a", "bufB:b", "bufE:a", "bufE:b"],
                      writes=["ps_scr:0"])

                def sc_o1(e, sb=sb):
                    for h in (9, 11, 13, 15):
                        last = sc_head(e, h, sb)
                    return last
                cstep(PE, sc_o1, reads=["bufB:a", "bufB:b", "bufE:a", "bufE:b"],
                      writes=["ps_scr:1"])

                for w in range(2):
                    cstep(VE, lambda e, sb=sb, w=w: e.scalar_tensor_tensor(
                        ssb3[:, 8 * w:8 * w + 8, :], scr3[:, 8 * w:8 * w + 8, :],
                        iscol[:, sb:sb + 1],
                        smask[:, :].unsqueeze(1).broadcast_to([128, 8, 128]),
                        op0=ALU.mult, op1=ALU.mult),
                        reads=["ps_scr:%d" % w, "smask", "iscol"],
                        writes=["scsb:%d" % w])

                def supd(e, sb=sb, g=g):
                    for h in range(NH):
                        p, half = h // 2, h % 2
                        v = bufF[:, D * sb + 64 * h:D * sb + 64 * h + 64]
                        last = e.matmul(ps_S[64 * half:64 * half + 64, 64 * p:64 * p + 64],
                                        v, v, start=(g == 0 and h < 2), stop=True,
                                        skip_group_check=not (g == 0 and h == 0),
                                        tile_position=(0, 64 * half))
                    return last
                cstep(PE, supd, reads=["bufF"], writes=["ps_S"])

                def attmm(e, sb=sb, g=g):
                    if g > 0:
                        for h in range(NH):
                            p, half = h // 2, h % 2
                            cb = 128 * p
                            e.matmul(ps_att[64 * half:64 * half + 64, cb:cb + 128],
                                     sS[64 * half:64 * half + 64, 64 * p:64 * p + 64],
                                     bufE[64 * half:64 * half + 64,
                                          CH * p + SUB * sb:CH * p + SUB * sb + 128],
                                     start=(h % 8 < 2), stop=False,
                                     skip_group_check=(half == 1),
                                     tile_position=(64 * half, 64 * half))
                    for h in range(NH):
                        p, half = h // 2, h % 2
                        cb = 128 * p
                        last = e.matmul(
                            ps_att[64 * half:64 * half + 64, cb:cb + 128],
                            bufF[:, D * sb + 64 * h:D * sb + 64 * h + 64],
                            scsb[:, 128 * h:128 * h + 128],
                            start=(g == 0 and h % 8 < 2), stop=(h % 8 in (6, 7)),
                            skip_group_check=(half == 1),
                            tile_position=(0, 64 * half))
                    return last
                cstep(PE, attmm, reads=["scsb:0", "scsb:1", "bufF",
                                        "bufE:a", "bufE:b", "sS"],
                      writes=["psA0", "psA1"])
                cstep(SC, lambda e: e.copy(sS[:, :], ps_S[:, :]),
                      reads=["ps_S"], writes=["sS"])

                def h2ev(e, sb=sb):
                    for half in range(2):
                        aslc = bufA[:, :].rearrange("p (f s q) -> p f s q",
                                                    s=NSUB, q=SUB)[:, 4 * half:4 * half + 4, sb, :]
                        cslc = bufC[:, :].rearrange("p (f s q) -> p f s q",
                                                    s=NSUB, q=SUB)[:, 4 * half:4 * half + 4, sb, :]
                        pslc = ps_att[:, 512 * half:512 * (half + 1)].rearrange(
                            "p (pp q) -> p pp q", q=SUB)
                        last = e.tensor_tensor(aslc, cslc, pslc, op=ALU.add)
                    return last
                cstep(VE, h2ev, reads=["psA0", "psA1", "bufC:a", "bufC:b"],
                      writes=["bufA:a", "bufA:b"])

            layer_norm(bufA, "bufA", bufB, "bufB")
            dense(sMh, sMl, ["sMh", "sMl"], bufC, "bufC", bufA, "bufA", True)

            dstep(GP, [(d_out[128 * i:128 * (i + 1), tc:tc + CH], fb(bufC, i))
                       for i in range(NFB)], reads=["bufC:a", "bufC:b"])

        @DEC["gp"]
        def _(e):
            for s in dpool["gp"]:
                if s.count:
                    e.wait_ge(s.h, s.count)
            for s in dpool["sy"]:
                if s.count:
                    e.wait_ge(s.h, s.count)

    return nc


_DFT_CACHE = {}


def _consts():
    if "mask" not in _DFT_CACHE:
        import ml_dtypes
        bf = ml_dtypes.bfloat16
        k = np.arange(D, dtype=np.int64)
        m = np.mod(np.outer(k, k), D).astype(np.float64)
        ang = (2.0 * np.pi / D) * m
        for nm, tab in (("cos", np.cos(ang)), ("sin", -np.sin(ang))):
            hi = tab.astype(bf)
            lo = (tab - hi.astype(np.float64)).astype(bf)
            _DFT_CACHE[nm + "h"] = hi
            _DFT_CACHE[nm + "l"] = lo
        q = np.arange(128)
        _DFT_CACHE["mask"] = (q[:, None] <= q[None, :]).astype(np.float32)
        _DFT_CACHE["ident"] = np.eye(128, dtype=np.float32)
    return _DFT_CACHE


def kernel(L0_attn_hidden_states, L0_attn_queries_weight,
           L0_ffn_w_freq_real, L0_ffn_w_freq_imag, L0_ffn_bias):
    x = np.asarray(L0_attn_hidden_states, dtype=np.float32)
    wq = np.ascontiguousarray(np.asarray(L0_attn_queries_weight, dtype=np.float32))
    wrt = np.ascontiguousarray(np.asarray(L0_ffn_w_freq_real, dtype=np.float32).T)
    wit = np.ascontiguousarray(np.asarray(L0_ffn_w_freq_imag, dtype=np.float32).T)
    bias = np.ascontiguousarray(np.asarray(L0_ffn_bias, dtype=np.float32))
    cc = _consts()

    nc = build_program()
    in_maps = []
    for b in range(8):
        in_maps.append({
            "xT": np.ascontiguousarray(x[b].T),
            "wq": wq, "wrt": wrt, "wit": wit, "bias": bias,
            "cosh": cc["cosh"], "cosl": cc["cosl"],
            "sinh": cc["sinh"], "sinl": cc["sinl"],
            "mask128": cc["mask"], "ident": cc["ident"],
        })
    res = run_bass_kernel_spmd(nc, in_maps, core_ids=list(range(8)))
    out = np.empty((8, T, D), dtype=np.float32)
    for b in range(8):
        out[b] = res.results[b]["outT"].T
    return out


# version 14
# speedup vs baseline: 1.1199x; 1.1199x over previous
import contextlib
import numpy as np

import concourse.bass as bass
import concourse.mybir as mybir
from concourse.bass_utils import run_bass_kernel_spmd

F32 = mybir.dt.float32
BF16 = mybir.dt.bfloat16
F32R = mybir.dt.float32r
AF = mybir.ActivationFunctionType
ALU = mybir.AluOpType

D = 1024
T = 4096
CH = 512
NCH = T // CH
SUB = 128
NSUB = CH // SUB
NH = 16
HD = 64
NFB = D // 128
EPS = 1e-5


class _Sem:
    __slots__ = ("h", "count")

    def __init__(self):
        self.h = None
        self.count = 0


class _Eng:
    __slots__ = ("name", "sem", "count")

    def __init__(self, name):
        self.name = name
        self.sem = None
        self.count = 0


def build_program():
    nc = bass.Bass("TRN2", target_bir_lowering=False)

    d_xT = nc.dram_tensor("xT", [D, T], F32, kind="ExternalInput")
    d_wq = nc.dram_tensor("wq", [D, D], F32, kind="ExternalInput")
    d_wrt = nc.dram_tensor("wrt", [D, D], F32, kind="ExternalInput")
    d_wit = nc.dram_tensor("wit", [D, D], F32, kind="ExternalInput")
    d_bias = nc.dram_tensor("bias", [D], F32, kind="ExternalInput")
    d_cosh = nc.dram_tensor("cosh", [D, D], BF16, kind="ExternalInput")
    d_cosl = nc.dram_tensor("cosl", [D, D], BF16, kind="ExternalInput")
    d_sinh = nc.dram_tensor("sinh", [D, D], BF16, kind="ExternalInput")
    d_sinl = nc.dram_tensor("sinl", [D, D], BF16, kind="ExternalInput")
    d_mask = nc.dram_tensor("mask128", [128, 128], F32, kind="ExternalInput")
    d_id = nc.dram_tensor("ident", [128, 128], F32, kind="ExternalInput")
    d_out = nc.dram_tensor("outT", [D, T], F32, kind="ExternalOutput")

    PE, VE, SC, SY, GP = (_Eng(n) for n in ("pe", "ve", "sc", "sy", "gp"))
    ALL = [PE, VE, SC, SY, GP]

    last_w = {}
    readers = {}

    def _deps(reads, writes):
        dep = {}

        def add(tok):
            if tok:
                e, v = tok
                if dep.get(e, 0) < v:
                    dep[e] = v
        for r in reads:
            add(last_w.get(r))
        for r in writes:
            add(last_w.get(r))
            for e, v in readers.get(r, {}).items():
                add((e, v))
        return dep

    def _commit(tok_src, reads, writes):
        tok = (tok_src, tok_src.count)
        for r in writes:
            last_w[r] = tok
            readers[r] = {}
        for r in reads:
            readers.setdefault(r, {})[tok_src] = tok_src.count

    def _sem_of(x):
        return x.sem if isinstance(x, _Eng) else x.h

    with contextlib.ExitStack() as S0:
        E0 = S0.enter_context
        ps_S = E0(nc.psum_tensor("ps_S", [128, 512], F32))
        ps_scr = E0(nc.psum_tensor("ps_scr", [128, 2048], F32))
        ps_att = E0(nc.psum_tensor("ps_att", [128, 1024], F32))
        ps_rot = E0(nc.psum_tensor("ps_rot", [128, 512], F32))
        for e in ALL:
            e.sem = E0(nc.semaphore("sem_" + e.name))
        _pool_n = [0]

        def E0s():
            s = _Sem()
            _pool_n[0] += 1
            s.h = E0(nc.semaphore("dsem_%d" % _pool_n[0]))
            return s
        block = E0(nc.Block())
        DEC = {"pe": block.tensor, "ve": block.vector, "sc": block.scalar,
               "sy": block.sync, "gp": block.gpsimd}

        NPOOL = 10
        dpool = {"sy": [E0s() for _ in range(NPOOL)],
                 "gp": [E0s() for _ in range(NPOOL)]}
        dpool_i = {"sy": 0, "gp": 0}

        def cstep(eng, emit, reads=(), writes=()):
            dep = _deps(reads, writes)

            @DEC[eng.name]
            def _(e):
                for o, v in dep.items():
                    e.wait_ge(_sem_of(o), v)
                emit(e).then_inc(eng.sem, 1)
            eng.count += 1
            _commit(eng, reads, writes)

        def dstep(eng, transfers, reads=(), writes=()):
            dep = _deps(reads, writes)
            sem = dpool[eng.name][dpool_i[eng.name] % NPOOL]
            dpool_i[eng.name] += 1
            prior = sem.count

            @DEC[eng.name]
            def _(e):
                for o, v in dep.items():
                    e.wait_ge(_sem_of(o), v)
                if prior:
                    e.wait_ge(sem.h, prior)
                for tr_ in transfers:
                    dst, src = tr_[0], tr_[1]
                    kw = tr_[2] if len(tr_) > 2 else {}
                    e.dma_start(dst, src, **kw).then_inc(sem.h, 16)
            sem.count += 16 * len(transfers)
            _commit(sem, reads, writes)

        slots = [(None, "psR"), (None, "psA0"), (None, "psA1")]

        sMh = E0(nc.sbuf_tensor("sMh", [128, 8192], BF16))
        sMl = E0(nc.sbuf_tensor("sMl", [128, 8192], BF16))
        s_ones = E0(nc.sbuf_tensor("s_ones", [128, 128], F32))
        slots[0] = (ps_rot[:, 0:512], "psR")
        slots[1] = (ps_att[:, 0:512], "psA0")
        slots[2] = (ps_att[:, 512:1024], "psA1")

        cstep(VE, lambda e: e.memset(s_ones[:, :], 1.0), writes=["s_ones"])
        cstep(SC, lambda e: e.preload_act_table(AF.Sqrt), writes=["acttab"])

        with contextlib.ExitStack() as S1:
            E1 = S1.enter_context
            sCh = E1(nc.sbuf_tensor("sCh", [128, 8192], BF16))
            sCl = E1(nc.sbuf_tensor("sCl", [128, 8192], BF16))
            sNh = E1(nc.sbuf_tensor("sNh", [128, 8192], BF16))
            sNl = E1(nc.sbuf_tensor("sNl", [128, 8192], BF16))
            sRh = E1(nc.sbuf_tensor("sRh", [128, 8192], BF16))
            sRl = E1(nc.sbuf_tensor("sRl", [128, 8192], BF16))
            sIh = E1(nc.sbuf_tensor("sIh", [128, 8192], BF16))
            sIl = E1(nc.sbuf_tensor("sIl", [128, 8192], BF16))
            sWtmp = E1(nc.sbuf_tensor("sWtmp", [128, 8192], F32))

            for src_, dst, rn in ((d_cosh, sCh, "sCh"), (d_cosl, sCl, "sCl"),
                                  (d_sinh, sNh, "sNh"), (d_sinl, sNl, "sNl")):
                dstep(SY, [(dst[:, 1024 * j:1024 * (j + 1)],
                            src_[128 * j:128 * (j + 1), :]) for j in range(8)],
                      writes=[rn])

            for dsrc, hi, lo, hn, ln_ in ((d_wrt, sRh, sRl, "sRh", "sRl"),
                                          (d_wit, sIh, sIl, "sIh", "sIl")):
                dstep(SY, [(sWtmp[:, 1024 * j:1024 * (j + 1)],
                            dsrc[128 * j:128 * (j + 1), :]) for j in range(8)],
                      writes=["sWtmp"])

                def caph(e, hi=hi):
                    for j in range(8):
                        last = e.activation(hi[:, 1024 * j:1024 * (j + 1)],
                                            sWtmp[:, 1024 * j:1024 * (j + 1)], AF.Identity)
                    return last
                cstep(SC, caph, reads=["sWtmp", "acttab"], writes=[hn])

                def capl(e, hi=hi, lo=lo):
                    for j in range(8):
                        last = e.scalar_tensor_tensor(
                            lo[:, 1024 * j:1024 * (j + 1)],
                            sWtmp[:, 1024 * j:1024 * (j + 1)], 1.0,
                            hi[:, 1024 * j:1024 * (j + 1)],
                            op0=ALU.mult, op1=ALU.subtract)
                    return last
                cstep(VE, capl, reads=["sWtmp", hn], writes=[ln_])

            rh3 = sRh[:, :].rearrange("p (ob d) -> p ob d", d=1024)
            rl3 = sRl[:, :].rearrange("p (ob d) -> p ob d", d=1024)
            ih3 = sIh[:, :].rearrange("p (ob d) -> p ob d", d=1024)
            il3 = sIl[:, :].rearrange("p (ob d) -> p ob d", d=1024)
            pslots = [(ps_rot[:, 0:512], "psR"), (ps_att[:, 0:512], "psA0"),
                      (ps_att[:, 512:1024], "psA1"), (ps_scr[:, 0:512], "psS0")]

            def tri(e, slot, ah, al, bh, bl, first, last_):
                for (x, y, st, sp) in ((ah, bh, first, False),
                                       (ah, bl, False, False),
                                       (al, bh, False, last_)):
                    for ob in range(8):
                        r = e.matmul(slot, x(ob), y(ob),
                                     start=(st and ob == 0),
                                     stop=(sp and ob == 7))
                return r

            for j in range(8):
                def mmPr(e, j=j):
                    for half in range(2):
                        wr_h = lambda ob, j=j: sRh[:, 1024 * ob + 128 * j:1024 * ob + 128 * (j + 1)]
                        wr_l = lambda ob, j=j: sRl[:, 1024 * ob + 128 * j:1024 * ob + 128 * (j + 1)]
                        wi_h = lambda ob, j=j: sIh[:, 1024 * ob + 128 * j:1024 * ob + 128 * (j + 1)]
                        wi_l = lambda ob, j=j: sIl[:, 1024 * ob + 128 * j:1024 * ob + 128 * (j + 1)]
                        c_h = lambda ob, half=half: sCh[:, 1024 * ob + 512 * half:1024 * ob + 512 * (half + 1)]
                        c_l = lambda ob, half=half: sCl[:, 1024 * ob + 512 * half:1024 * ob + 512 * (half + 1)]
                        n_h = lambda ob, half=half: sNh[:, 1024 * ob + 512 * half:1024 * ob + 512 * (half + 1)]
                        n_l = lambda ob, half=half: sNl[:, 1024 * ob + 512 * half:1024 * ob + 512 * (half + 1)]
                        tri(e, pslots[half][0], wr_h, wr_l, c_h, c_l, True, False)
                        last = tri(e, pslots[half][0], wi_h, wi_l, n_h, n_l, False, True)
                    return last
                cstep(PE, mmPr, reads=["sRh", "sRl", "sIh", "sIl",
                                       "sCh", "sCl", "sNh", "sNl"],
                      writes=["psR", "psA0"])

                cstep(VE, lambda e, j=j: (e.tensor_scalar_mul(
                    ih3[:, :, 128 * j:128 * (j + 1)],
                    ih3[:, :, 128 * j:128 * (j + 1)], -1.0),
                    e.tensor_scalar_mul(
                    il3[:, :, 128 * j:128 * (j + 1)],
                    il3[:, :, 128 * j:128 * (j + 1)], -1.0))[-1],
                    reads=["sIh", "sIl"], writes=["sIh", "sIl"])

                def mmPi(e, j=j):
                    for half in range(2):
                        wr_h = lambda ob, j=j: sRh[:, 1024 * ob + 128 * j:1024 * ob + 128 * (j + 1)]
                        wr_l = lambda ob, j=j: sRl[:, 1024 * ob + 128 * j:1024 * ob + 128 * (j + 1)]
                        wi_h = lambda ob, j=j: sIh[:, 1024 * ob + 128 * j:1024 * ob + 128 * (j + 1)]
                        wi_l = lambda ob, j=j: sIl[:, 1024 * ob + 128 * j:1024 * ob + 128 * (j + 1)]
                        c_h = lambda ob, half=half: sCh[:, 1024 * ob + 512 * half:1024 * ob + 512 * (half + 1)]
                        c_l = lambda ob, half=half: sCl[:, 1024 * ob + 512 * half:1024 * ob + 512 * (half + 1)]
                        n_h = lambda ob, half=half: sNh[:, 1024 * ob + 512 * half:1024 * ob + 512 * (half + 1)]
                        n_l = lambda ob, half=half: sNl[:, 1024 * ob + 512 * half:1024 * ob + 512 * (half + 1)]
                        tri(e, pslots[2 + half][0], wi_h, wi_l, c_h, c_l, True, False)
                        last = tri(e, pslots[2 + half][0], wr_h, wr_l, n_h, n_l, False, True)
                    return last
                cstep(PE, mmPi, reads=["sRh", "sRl", "sIh", "sIl",
                                       "sCh", "sCl", "sNh", "sNl"],
                      writes=["psA1", "psS0"])

                def evh(e, j=j):
                    for half in range(2):
                        e.activation(rh3[:, 4 * half:4 * half + 4, 128 * j:128 * (j + 1)],
                                     pslots[half][0].rearrange("p (o c) -> p o c", c=128),
                                     AF.Identity)
                        last = e.activation(ih3[:, 4 * half:4 * half + 4, 128 * j:128 * (j + 1)],
                                            pslots[2 + half][0].rearrange("p (o c) -> p o c", c=128),
                                            AF.Identity)
                    return last
                cstep(SC, evh, reads=["psR", "psA0", "psA1", "psS0", "acttab"],
                      writes=["sRh", "sIh"])

                def evl(e, j=j):
                    for half in range(2):
                        e.scalar_tensor_tensor(
                            rl3[:, 4 * half:4 * half + 4, 128 * j:128 * (j + 1)],
                            pslots[half][0].rearrange("p (o c) -> p o c", c=128), 1.0,
                            rh3[:, 4 * half:4 * half + 4, 128 * j:128 * (j + 1)],
                            op0=ALU.mult, op1=ALU.subtract)
                        last = e.scalar_tensor_tensor(
                            il3[:, 4 * half:4 * half + 4, 128 * j:128 * (j + 1)],
                            pslots[2 + half][0].rearrange("p (o c) -> p o c", c=128), 1.0,
                            ih3[:, 4 * half:4 * half + 4, 128 * j:128 * (j + 1)],
                            op0=ALU.mult, op1=ALU.subtract)
                    return last
                cstep(VE, evl, reads=["psR", "psA0", "psA1", "psS0", "sRh", "sIh"],
                      writes=["sRl", "sIl"])

            for ab in range(8):
                for half in range(2):
                    slot, sn = slots[(2 * ab + half) % 3]

                    def mmM(e, ab=ab, half=half, slot=slot):
                        c_h = lambda kb, ab=ab: sCh[:, 1024 * kb + 128 * ab:1024 * kb + 128 * (ab + 1)]
                        c_l = lambda kb, ab=ab: sCl[:, 1024 * kb + 128 * ab:1024 * kb + 128 * (ab + 1)]
                        n_h = lambda kb, ab=ab: sNh[:, 1024 * kb + 128 * ab:1024 * kb + 128 * (ab + 1)]
                        n_l = lambda kb, ab=ab: sNl[:, 1024 * kb + 128 * ab:1024 * kb + 128 * (ab + 1)]
                        pr_h = lambda kb, half=half: rh3[:, 4 * half:4 * half + 4, 128 * kb:128 * (kb + 1)]
                        pr_l = lambda kb, half=half: rl3[:, 4 * half:4 * half + 4, 128 * kb:128 * (kb + 1)]
                        pi_h = lambda kb, half=half: ih3[:, 4 * half:4 * half + 4, 128 * kb:128 * (kb + 1)]
                        pi_l = lambda kb, half=half: il3[:, 4 * half:4 * half + 4, 128 * kb:128 * (kb + 1)]
                        tri(e, slot, c_h, c_l, pr_h, pr_l, True, False)
                        return tri(e, slot, n_h, n_l, pi_h, pi_l, False, True)
                    cstep(PE, mmM, reads=["sCh", "sCl", "sNh", "sNl",
                                          "sRh", "sRl", "sIh", "sIl"], writes=[sn])
                    cstep(SC, lambda e, ab=ab, half=half, slot=slot: e.activation(
                        sMh[:, 1024 * ab + 512 * half:1024 * ab + 512 * (half + 1)],
                        slot, AF.Identity, scale=1.0 / D),
                        reads=[sn, "acttab"], writes=["sMh"])
                    cstep(VE, lambda e, ab=ab, half=half, slot=slot: e.scalar_tensor_tensor(
                        sMl[:, 1024 * ab + 512 * half:1024 * ab + 512 * (half + 1)],
                        slot, 1.0 / D,
                        sMh[:, 1024 * ab + 512 * half:1024 * ab + 512 * (half + 1)],
                        op0=ALU.mult, op1=ALU.subtract),
                        reads=[sn, "sMh"], writes=["sMl"])

        prep_res = ("sCh", "sCl", "sNh", "sNl", "sRh", "sRl", "sIh", "sIl", "sWtmp")
        bar = _deps(prep_res, prep_res)
        steady_names = ["swqh", "swql", "sbias", "smask", "sid", "sS", "murep", "isrep",
                        "ftmp0", "ftmp1", "sqt0", "sqt1", "sqp0", "sqp1",
                        "bufA:a", "bufA:b", "bufA2:a", "bufA2:b",
                        "bufB:a", "bufB:b", "bufC:a", "bufC:b",
                        "bufE:a", "bufE:b", "bufF",
                        "bufH:a", "bufH:b", "bufI:a", "bufI:b",
                        "scsb:0", "scsb:1", "iscol"]
        for rname in steady_names:
            readers[rname] = dict(bar)

        swqh = E0(nc.sbuf_tensor("swqh", [128, 8192], BF16))
        swql = E0(nc.sbuf_tensor("swql", [128, 8192], BF16))
        sbias = E0(nc.sbuf_tensor("sbias", [128, 8], F32))
        smask = E0(nc.sbuf_tensor("smask", [128, 128], F32))
        sid = E0(nc.sbuf_tensor("sid", [128, 128], F32))
        sS = E0(nc.sbuf_tensor("sS", [128, 512], F32))
        murep = E0(nc.sbuf_tensor("murep", [128, 512], F32))
        isrep = E0(nc.sbuf_tensor("isrep", [128, 512], F32))
        ftmp = [E0(nc.sbuf_tensor("ftmp0", [128, 512], F32)),
                E0(nc.sbuf_tensor("ftmp1", [128, 512], F32))]
        sqt0 = E0(nc.sbuf_tensor("sqt0", [128, 512], F32))
        sqt1 = E0(nc.sbuf_tensor("sqt1", [128, 512], F32))
        sqp = [E0(nc.sbuf_tensor("sqp0", [128, 512], F32)),
               E0(nc.sbuf_tensor("sqp1", [128, 512], F32))]
        bufA = E0(nc.sbuf_tensor("bufA", [128, NFB * CH], F32))
        bufB = E0(nc.sbuf_tensor("bufB", [128, NFB * CH], F32))
        bufC = E0(nc.sbuf_tensor("bufC", [128, NFB * CH], F32))
        bufE = E0(nc.sbuf_tensor("bufE", [128, NFB * CH], F32))
        bufF = E0(nc.sbuf_tensor("bufF", [128, NSUB * D], F32))
        bufA2 = E0(nc.sbuf_tensor("bufA2", [128, NFB * CH], F32))
        bufH = E0(nc.sbuf_tensor("bufH", [128, NFB * CH], BF16))
        bufI = E0(nc.sbuf_tensor("bufI", [128, NFB * CH], BF16))
        scsb = E0(nc.sbuf_tensor("scsb", [128, 2048], F32))
        iscol = E0(nc.sbuf_tensor("iscol", [128, NSUB], F32))

        dstep(SY, [(bufA[:, 1024 * j:1024 * (j + 1)], d_wq[128 * j:128 * (j + 1), :])
                   for j in range(4)], writes=["bufA:a", "bufA:b"])
        dstep(SY, [(bufB[:, 1024 * j:1024 * (j + 1)], d_wq[128 * (4 + j):128 * (5 + j), :])
                   for j in range(4)], writes=["bufB:a", "bufB:b"])

        def wqh(e):
            for j in range(8):
                srcb = (bufA if j < 4 else bufB)[:, 1024 * (j % 4):1024 * (j % 4 + 1)]
                last = e.activation(swqh[:, 1024 * j:1024 * (j + 1)], srcb, AF.Identity)
            return last
        cstep(SC, wqh, reads=["bufA:a", "bufA:b", "bufB:a", "bufB:b", "acttab"],
              writes=["swqh"])

        def wql(e):
            for j in range(8):
                srcb = (bufA if j < 4 else bufB)[:, 1024 * (j % 4):1024 * (j % 4 + 1)]
                last = e.scalar_tensor_tensor(
                    swql[:, 1024 * j:1024 * (j + 1)], srcb, 1.0,
                    swqh[:, 1024 * j:1024 * (j + 1)], op0=ALU.mult, op1=ALU.subtract)
            return last
        cstep(VE, wql, reads=["bufA:a", "bufA:b", "bufB:a", "bufB:b", "swqh"],
              writes=["swql"])

        dstep(SY, [(sbias[:, :], d_bias.ap().rearrange("(b p) -> p b", p=128),
                    {"allow_slow_non_contiguous": True}),
                   (smask[:, :], d_mask.ap()),
                   (sid[:, :], d_id.ap())],
              writes=["sbias", "smask", "sid"])

        cstep(VE, lambda e: e.tensor_scalar_mul(sbias[:, :], sbias[:, :], 0.5),
              reads=["sbias"], writes=["sbias"])
        cstep(VE, lambda e: e.memset(scsb[:, :], 0.0),
              writes=["scsb:0", "scsb:1"])
        cstep(VE, lambda e: e.memset(ps_scr[:, :], 0.0),
              writes=["ps_scr:0", "ps_scr:1"])

        def fb(buf, i, t0=0, t1=CH):
            return buf[:, CH * i + t0:CH * i + t1]

        def layer_norm(zF, zn, nF, nn, make_iscol=False):
            z4 = zF[:, :].rearrange("p (f t) -> p t f", t=CH)
            cstep(VE, lambda e: e.tensor_reduce(
                sqt0[:, :].unsqueeze(2), z4, axis=mybir.AxisListType.X, op=ALU.add),
                reads=[zn + ":a", zn + ":b"], writes=["sqt0"])
            cstep(PE, lambda e: e.matmul(ps_rot[:, 0:512], s_ones[:, :], sqt0[:, :],
                                         start=True, stop=True),
                  reads=["sqt0", "s_ones"], writes=["psR"])
            for i in range(NFB):
                tag = ":" + ("a" if i < 4 else "b")
                cstep(SC, lambda e, i=i: e.activation(
                    sqp[i % 2][:, :], fb(zF, i), AF.Square),
                    reads=[zn + tag, "acttab"], writes=["sqp%d" % (i % 2)])
                if i == 0:
                    cstep(VE, lambda e: e.tensor_copy(sqt1[:, :], sqp[0][:, :]),
                          reads=["sqp0"], writes=["sqt1"])
                else:
                    cstep(VE, lambda e, i=i: e.tensor_add(
                        sqt1[:, :], sqt1[:, :], sqp[i % 2][:, :]),
                        reads=["sqp%d" % (i % 2), "sqt1"], writes=["sqt1"])
            cstep(PE, lambda e: e.matmul(ps_att[:, 0:512], s_ones[:, :], sqt1[:, :],
                                         start=True, stop=True),
                  reads=["sqt1", "s_ones"], writes=["psA0"])

            cstep(VE, lambda e: e.tensor_scalar_mul(murep[:, :], ps_rot[:, 0:512], 1.0 / D),
                  reads=["psR"], writes=["murep"])
            cstep(VE, lambda e: e.tensor_tensor(isrep[:, :], murep[:, :], murep[:, :],
                                                op=ALU.mult),
                  reads=["murep"], writes=["isrep"])
            cstep(VE, lambda e: e.scalar_tensor_tensor(
                isrep[:, :], ps_att[:, 0:512], 1.0 / D, isrep[:, :],
                op0=ALU.mult, op1=ALU.subtract),
                reads=["psA0", "isrep"], writes=["isrep"])
            cstep(VE, lambda e: e.tensor_scalar_add(isrep[:, :], isrep[:, :], EPS),
                  reads=["isrep"], writes=["isrep"])
            cstep(VE, lambda e: e.reciprocal(isrep[:, :], isrep[:, :]),
                  reads=["isrep"], writes=["isrep"])
            cstep(SC, lambda e: e.activation(isrep[:, :], isrep[:, :], AF.Sqrt),
                  reads=["isrep", "acttab"], writes=["isrep"])

            z3 = zF[:, :].rearrange("p (f t) -> p f t", t=CH)
            n3 = nF[:, :].rearrange("p (f t) -> p f t", t=CH)
            mrep = murep[:, :].unsqueeze(1).broadcast_to([128, 4, CH])
            for h_ in range(2):
                s_ = slice(4 * h_, 4 * h_ + 4)
                tag = ":" + "ab"[h_]
                cstep(VE, lambda e, s_=s_: e.tensor_tensor(
                    n3[:, s_, :], z3[:, s_, :], mrep, op=ALU.subtract),
                    reads=[zn + tag, "murep"], writes=[nn + tag])
                cstep(SC, lambda e, h_=h_: e.activation(
                    bufH[:, 2048 * h_:2048 * (h_ + 1)],
                    nF[:, 2048 * h_:2048 * (h_ + 1)], AF.Identity),
                    reads=[nn + tag, "acttab"], writes=["bufH" + tag])
                cstep(VE, lambda e, h_=h_: e.scalar_tensor_tensor(
                    bufI[:, 2048 * h_:2048 * (h_ + 1)],
                    nF[:, 2048 * h_:2048 * (h_ + 1)], 1.0,
                    bufH[:, 2048 * h_:2048 * (h_ + 1)],
                    op0=ALU.mult, op1=ALU.subtract),
                    reads=[nn + tag, "bufH" + tag], writes=["bufI" + tag])
            if make_iscol:
                def istr(e):
                    for s in range(NSUB):
                        last = e.matmul(ps_scr[:, s:s + 1],
                                        isrep[0:1, SUB * s:SUB * (s + 1)],
                                        sid[0:1, 0:1], is_transpose=True,
                                        start=(s == 0), stop=(s == NSUB - 1))
                    return last
                cstep(PE, istr, reads=["isrep", "sid"], writes=["ps_scr:0"])
                cstep(SC, lambda e: e.copy(iscol[:, :], ps_scr[:, 0:NSUB]),
                      reads=["ps_scr:0", "acttab"], writes=["iscol"])

        def dense(wh, wl, wns, outF, on, resF, rn, with_bias_half):
            for db in range(NFB):
                slot, sn = slots[db % 3]
                otag = on + (":a" if db < 4 else ":b")

                def mm1(e, db=db, slot=slot):
                    for kb in range(4):
                        last = e.matmul(
                            slot,
                            wh[:, 1024 * kb + 128 * db:1024 * kb + 128 * (db + 1)],
                            fb(bufH, kb), start=(kb == 0), stop=False)
                    return last
                cstep(PE, mm1, reads=["bufH:a"] + wns, writes=[sn])

                def mm2(e, db=db, slot=slot):
                    for kb in range(4, 8):
                        e.matmul(slot,
                                 wh[:, 1024 * kb + 128 * db:1024 * kb + 128 * (db + 1)],
                                 fb(bufH, kb), start=False, stop=False)
                    for wt, nt, sp in ((wh, bufI, False), (wl, bufH, True)):
                        for kb in range(NFB):
                            last = e.matmul(
                                slot,
                                wt[:, 1024 * kb + 128 * db:1024 * kb + 128 * (db + 1)],
                                fb(nt, kb), start=False,
                                stop=(sp and kb == NFB - 1))
                    return last
                cstep(PE, mm2, reads=["bufH:a", "bufH:b", "bufI:a", "bufI:b"] + wns,
                      writes=[sn])
                if with_bias_half:
                    ft, fn = ftmp[db % 2], "ftmp%d" % (db % 2)
                    cstep(VE, lambda e, db=db, slot=slot, ft=ft: e.scalar_tensor_tensor(
                        ft[:, :], slot, 0.5, isrep[:, :], op0=ALU.mult, op1=ALU.mult),
                        reads=[sn, "isrep"], writes=[fn])
                    cstep(VE, lambda e, db=db, ft=ft: e.scalar_tensor_tensor(
                        fb(outF, db), ft[:, :], sbias[:, db:db + 1], fb(resF, db),
                        op0=ALU.add, op1=ALU.add),
                        reads=[fn, rn + (":a" if db < 4 else ":b"), "sbias"],
                        writes=[otag])
                else:
                    cstep(VE, lambda e, db=db, slot=slot: e.scalar_tensor_tensor(
                        fb(outF, db), slot, 1.0, isrep[:, :], op0=ALU.mult, op1=ALU.mult),
                        reads=[sn, "isrep"], writes=[otag])

        for c in range(NCH):
            tc = CH * c
            resA = bufA if c % 2 == 0 else bufA2
            rn_ = "bufA" if c % 2 == 0 else "bufA2"

            dstep(SY, [(fb(resA, i), d_xT[128 * i:128 * (i + 1), tc:tc + CH])
                       for i in range(NFB)], writes=[rn_ + ":a", rn_ + ":b"])

            layer_norm(resA, rn_, bufB, "bufB")
            dense(sMh, sMl, ["sMh", "sMl"], bufC, "bufC", resA, rn_, True)

            layer_norm(bufC, "bufC", bufB, "bufB", make_iscol=True)
            dense(swqh, swql, ["swqh", "swql"], bufE, "bufE", None, None, False)

            for sb in range(NSUB):
                def tr(e, sb=sb):
                    for i in range(NFB):
                        last = e.matmul(
                            ps_scr[:, 128 * i:128 * (i + 1)],
                            bufB[:, CH * i + SUB * sb:CH * i + SUB * (sb + 1)],
                            sid[:, :], is_transpose=True,
                            start=(i % 4 == 0), stop=(i % 4 == 3))
                    return last
                cstep(PE, tr, reads=["bufB:a", "bufB:b", "sid"], writes=["ps_scr:0"])

                def trev(e, sb=sb):
                    e.activation(bufF[:, D * sb:D * sb + 512], ps_scr[:, 0:512],
                                 AF.Identity, scale=iscol[:, sb:sb + 1])
                    return e.activation(bufF[:, D * sb + 512:D * sb + 1024],
                                        ps_scr[:, 512:1024],
                                        AF.Identity, scale=iscol[:, sb:sb + 1])
                cstep(SC, trev, reads=["ps_scr:0", "iscol", "acttab"], writes=["bufF"])

            scr3 = ps_scr[:, :].rearrange("p (h c) -> p h c", c=128)
            ssb3 = scsb[:, :].rearrange("p (h c) -> p h c", c=128)
            for sb in range(NSUB):
                g = NSUB * c + sb

                def sc_head(e, h, sb):
                    p, half = h // 2, h % 2
                    kv = bufB[64 * half:64 * half + 64,
                              CH * p + SUB * sb:CH * p + SUB * sb + 128]
                    q = bufE[64 * half:64 * half + 64,
                             CH * p + SUB * sb:CH * p + SUB * sb + 128]
                    return e.matmul(ps_scr[0:128, 128 * h:128 * h + 128], kv, q,
                                    start=(h % 4 == 0), stop=(h % 4 == 3),
                                    tile_position=(64 * half, 0))

                def sc_e(e, sb=sb):
                    for h in range(0, NH, 2):
                        last = sc_head(e, h, sb)
                    return last
                cstep(PE, sc_e, reads=["bufB:a", "bufB:b", "bufE:a", "bufE:b"],
                      writes=["ps_scr:0", "ps_scr:1"])

                def sc_o0(e, sb=sb):
                    e.drain()
                    for h in (1, 3, 5, 7):
                        last = sc_head(e, h, sb)
                    return last
                cstep(PE, sc_o0, reads=["bufB:a", "bufB:b", "bufE:a", "bufE:b"],
                      writes=["ps_scr:0"])

                def sc_o1(e, sb=sb):
                    for h in (9, 11, 13, 15):
                        last = sc_head(e, h, sb)
                    return last
                cstep(PE, sc_o1, reads=["bufB:a", "bufB:b", "bufE:a", "bufE:b"],
                      writes=["ps_scr:1"])

                for w in range(2):
                    cstep(VE, lambda e, sb=sb, w=w: e.scalar_tensor_tensor(
                        ssb3[:, 8 * w:8 * w + 8, :], scr3[:, 8 * w:8 * w + 8, :],
                        iscol[:, sb:sb + 1],
                        smask[:, :].unsqueeze(1).broadcast_to([128, 8, 128]),
                        op0=ALU.mult, op1=ALU.mult),
                        reads=["ps_scr:%d" % w, "smask", "iscol"],
                        writes=["scsb:%d" % w])

                def supd(e, sb=sb, g=g):
                    for h in range(NH):
                        p, half = h // 2, h % 2
                        v = bufF[:, D * sb + 64 * h:D * sb + 64 * h + 64]
                        last = e.matmul(ps_S[64 * half:64 * half + 64, 64 * p:64 * p + 64],
                                        v, v, start=(g == 0 and h < 2), stop=True,
                                        skip_group_check=not (g == 0 and h == 0),
                                        tile_position=(0, 64 * half))
                    return last
                cstep(PE, supd, reads=["bufF"], writes=["ps_S"])

                def attmm(e, sb=sb, g=g):
                    if g > 0:
                        for h in range(NH):
                            p, half = h // 2, h % 2
                            cb = 128 * p
                            e.matmul(ps_att[64 * half:64 * half + 64, cb:cb + 128],
                                     sS[64 * half:64 * half + 64, 64 * p:64 * p + 64],
                                     bufE[64 * half:64 * half + 64,
                                          CH * p + SUB * sb:CH * p + SUB * sb + 128],
                                     start=(h % 8 < 2), stop=False,
                                     skip_group_check=(half == 1),
                                     tile_position=(64 * half, 64 * half))
                    for h in range(NH):
                        p, half = h // 2, h % 2
                        cb = 128 * p
                        last = e.matmul(
                            ps_att[64 * half:64 * half + 64, cb:cb + 128],
                            bufF[:, D * sb + 64 * h:D * sb + 64 * h + 64],
                            scsb[:, 128 * h:128 * h + 128],
                            start=(g == 0 and h % 8 < 2), stop=(h % 8 in (6, 7)),
                            skip_group_check=(half == 1),
                            tile_position=(0, 64 * half))
                    return last
                cstep(PE, attmm, reads=["scsb:0", "scsb:1", "bufF",
                                        "bufE:a", "bufE:b", "sS"],
                      writes=["psA0", "psA1"])
                cstep(SC, lambda e: e.copy(sS[:, :], ps_S[:, :]),
                      reads=["ps_S"], writes=["sS"])

                def h2ev(e, sb=sb, resA=resA):
                    for half in range(2):
                        aslc = resA[:, :].rearrange("p (f s q) -> p f s q",
                                                    s=NSUB, q=SUB)[:, 4 * half:4 * half + 4, sb, :]
                        cslc = bufC[:, :].rearrange("p (f s q) -> p f s q",
                                                    s=NSUB, q=SUB)[:, 4 * half:4 * half + 4, sb, :]
                        pslc = ps_att[:, 512 * half:512 * (half + 1)].rearrange(
                            "p (pp q) -> p pp q", q=SUB)
                        last = e.tensor_tensor(aslc, cslc, pslc, op=ALU.add)
                    return last
                cstep(VE, h2ev, reads=["psA0", "psA1", "bufC:a", "bufC:b"],
                      writes=[rn_ + ":a", rn_ + ":b"])

            layer_norm(resA, rn_, bufB, "bufB")
            dense(sMh, sMl, ["sMh", "sMl"], bufC, "bufC", resA, rn_, True)

            dstep(GP, [(d_out[128 * i:128 * (i + 1), tc:tc + CH], fb(bufC, i))
                       for i in range(NFB)], reads=["bufC:a", "bufC:b"])

        @DEC["gp"]
        def _(e):
            for s in dpool["gp"]:
                if s.count:
                    e.wait_ge(s.h, s.count)
            for s in dpool["sy"]:
                if s.count:
                    e.wait_ge(s.h, s.count)

    return nc


_DFT_CACHE = {}


def _consts():
    if "mask" not in _DFT_CACHE:
        import ml_dtypes
        bf = ml_dtypes.bfloat16
        k = np.arange(D, dtype=np.int64)
        m = np.mod(np.outer(k, k), D).astype(np.float64)
        ang = (2.0 * np.pi / D) * m
        for nm, tab in (("cos", np.cos(ang)), ("sin", -np.sin(ang))):
            hi = tab.astype(bf)
            lo = (tab - hi.astype(np.float64)).astype(bf)
            _DFT_CACHE[nm + "h"] = hi
            _DFT_CACHE[nm + "l"] = lo
        q = np.arange(128)
        _DFT_CACHE["mask"] = (q[:, None] <= q[None, :]).astype(np.float32)
        _DFT_CACHE["ident"] = np.eye(128, dtype=np.float32)
    return _DFT_CACHE


def kernel(L0_attn_hidden_states, L0_attn_queries_weight,
           L0_ffn_w_freq_real, L0_ffn_w_freq_imag, L0_ffn_bias):
    x = np.asarray(L0_attn_hidden_states, dtype=np.float32)
    wq = np.ascontiguousarray(np.asarray(L0_attn_queries_weight, dtype=np.float32))
    wrt = np.ascontiguousarray(np.asarray(L0_ffn_w_freq_real, dtype=np.float32).T)
    wit = np.ascontiguousarray(np.asarray(L0_ffn_w_freq_imag, dtype=np.float32).T)
    bias = np.ascontiguousarray(np.asarray(L0_ffn_bias, dtype=np.float32))
    cc = _consts()

    nc = build_program()
    in_maps = []
    for b in range(8):
        in_maps.append({
            "xT": np.ascontiguousarray(x[b].T),
            "wq": wq, "wrt": wrt, "wit": wit, "bias": bias,
            "cosh": cc["cosh"], "cosl": cc["cosl"],
            "sinh": cc["sinh"], "sinl": cc["sinl"],
            "mask128": cc["mask"], "ident": cc["ident"],
        })
    res = run_bass_kernel_spmd(nc, in_maps, core_ids=list(range(8)))
    out = np.empty((8, T, D), dtype=np.float32)
    for b in range(8):
        out[b] = res.results[b]["outT"].T
    return out
